# version 01 (final)
# baseline (speedup 1.0000x reference)
import contextlib
import numpy as np
import ml_dtypes
import concourse.bass as bass
import concourse.mybir as mybir
from concourse.bass_utils import run_bass_kernel_spmd

F32 = mybir.dt.float32
BF16 = mybir.dt.bfloat16
I8 = mybir.dt.int8
U8 = mybir.dt.uint8
ALU = mybir.AluOpType
AF = mybir.ActivationFunctionType
AX = mybir.AxisListType
BF = ml_dtypes.bfloat16

B, S, H = 4, 2048, 4096
NH, KVH, HD = 32, 8, 128
RD, CD = 512, 512
I_FFN = 11008
SCALE = float(1.0 / np.sqrt(128.0))
EPS = 1e-5
SLABS = [(0, 29), (29, 29), (58, 28)]
ESZ = {F32: 4, BF16: 2, I8: 1, U8: 1}

T, V, SC, SY, G = "tensor", "vector", "scalar", "sync", "gpsimd"


class Rec:

    def __init__(self):
        self.ops = {k: [] for k in (SY, G, SC, V, T)}
        self.semv = {}

    def on(self, eng, fn):
        self.ops[eng].append(fn)

    def inc(self, sem, n=1):
        self.semv[sem] = self.semv.get(sem, 0) + n
        return self.semv[sem]

    def cur(self, sem):
        return self.semv.get(sem, 0)

    def wait(self, eng, sem, val):
        if val is None or val <= 0:
            return
        self.on(eng, lambda e, s=sem, v=val: e.wait_ge(s, v))

    def waitp(self, eng, pair):
        if pair is not None and pair[0] is not None:
            self.wait(eng, pair[0], pair[1])

    def dma(self, eng, dst, src, sem):
        v = self.inc(sem, 16)
        self.on(eng, lambda e, d=dst, s=src, m=sem: e.dma_start(d, s).then_inc(m, 16))
        return v

    def op(self, eng, fn, sem=None):
        if sem is None:
            self.on(eng, fn)
            return None
        v = self.inc(sem, 1)
        self.on(eng, lambda e, f=fn, s=sem: f(e).then_inc(s, 1))
        return v

    def mark(self, eng, sem):
        v = self.inc(sem, 1)
        self.on(eng, lambda e, s=sem: e.nop().then_inc(s, 1))
        return v


class Zone:
    def __init__(self, arena, base, size, name):
        self.arena, self.base, self.size, self.name = arena, base, size, name
        self.off = 0

    def reset(self):
        self.off = 0

    def alloc(self, parts, elems, dt, align=64):
        nb = elems * ESZ[dt]
        o = (self.off + align - 1) // align * align
        self.off = o + nb
        assert self.off <= self.size, f"zone {self.name} overflow: {self.off} > {self.size}"
        a = self.base + o
        return self.arena[0:parts, a:a + nb].bitcast(dt)

    def sub(self, size, name):
        o = (self.off + 63) // 64 * 64
        self.off = o + size
        assert self.off <= self.size, f"zone {self.name} overflow (sub)"
        return Zone(self.arena, self.base + o, size, name)


def build(dbg=False, phases=9, races=True):
    nc = bass.Bass("TRN2", target_bir_lowering=False,
                   detect_race_conditions=races)
    ex = contextlib.ExitStack()

    def din(name, shape, dt):
        return nc.dram_tensor(name, shape, dt, kind="ExternalInput")

    d_hsT = din("hsT", [H, S], F32)
    d_hsTq = din("hsTq", [H, 1024], F32)
    d_hid = din("hid_own", [1024, H], F32)
    d_WqT = din("WqT", [H, H], F32)
    d_WkrT = din("WkrT", [H, RD], F32)
    d_WdkT = din("WdkT", [H, CD], F32)
    d_WupkT = din("WupkT", [CD, RD], F32)
    d_WupvT = din("WupvT", [CD, KVH * HD], F32)
    d_WoT = din("WoT", [H, H], F32)
    d_G8 = din("G8", [I_FFN, H], I8)
    d_U8 = din("U8", [I_FFN, H], I8)
    d_D8 = din("D8", [H, I_FFN], I8)
    d_gsc = din("gsc", [32, I_FFN], F32)
    d_usc = din("usc", [32, I_FFN], F32)
    d_dsc = din("dsc", [86, H], F32)
    d_gz = din("gz", [32, I_FFN], F32)
    d_uz = din("uz", [32, I_FFN], F32)
    d_dz = din("dz", [86, H], F32)
    d_cosKR = din("cosKR", [128, S], F32)
    d_sinKR = din("sinKR", [128, S], F32)
    d_cosQ = din("cosQ", [128, 1024], F32)
    d_sinQ = din("sinQ", [128, 1024], F32)
    d_idbf = din("idbf", [128, 128], BF16)
    d_idf32 = din("idf32", [128, 128], F32)
    d_mask0 = din("mask0", [128, 128], BF16)
    d_mask1 = din("mask1", [128, 128], BF16)
    d_oh32 = din("oh32", [128, 32 * 32], BF16)
    d_oh8 = din("oh8", [128, 8 * 8], BF16)

    d_out = nc.dram_tensor("out", [1024, H], F32, kind="ExternalOutput")

    d_qT = nc.dram_tensor("qT_spill", [NH * 128, 1024], BF16, kind="Internal")
    d_attn = nc.dram_tensor("attn_stage", [1024, H], BF16, kind="Internal")
    d_x1 = nc.dram_tensor("x1_stage", [1024, H], F32, kind="Internal")
    d_zsg = nc.dram_tensor("zsg", [32, I_FFN], BF16, kind="Internal")
    d_zsu = nc.dram_tensor("zsu", [32, I_FFN], BF16, kind="Internal")
    d_ypart = nc.dram_tensor("ypart", [H, 1024], F32, kind="Internal")
    d_ynat = nc.dram_tensor("y_nat", [1024, H], F32, kind="Internal")

    dbg_t = {}
    if dbg:
        dbg_t["kfull"] = nc.dram_tensor("dbg_kfull", [128, KVH * S], BF16, kind="ExternalOutput")
        dbg_t["vaug"] = nc.dram_tensor("dbg_vaug", [128, 16 * 8 * 136], BF16, kind="ExternalOutput")
        dbg_t["ctxT"] = nc.dram_tensor("dbg_ctxT", [128, NH * 1024], BF16, kind="ExternalOutput")
        dbg_t["qT"] = nc.dram_tensor("dbg_qT", [NH * 128, 1024], BF16, kind="ExternalOutput")
        dbg_t["O"] = nc.dram_tensor("dbg_O", [128, 8 * 136], F32, kind="ExternalOutput")
        dbg_t["pr"] = nc.dram_tensor("dbg_pr", [128, 16 * 128], BF16, kind="ExternalOutput")
        dbg_t["h0"] = nc.dram_tensor("dbg_h0", [128, 2048], BF16, kind="ExternalOutput")
        dbg_t["x1"] = nc.dram_tensor("dbg_x1", [1024, H], F32, kind="ExternalOutput")
        dbg_t["attn"] = nc.dram_tensor("dbg_attn", [1024, H], BF16, kind="ExternalOutput")

    AREN = 202 * 1024
    arena = ex.enter_context(nc.sbuf_tensor("arena", [128, AREN], U8))
    Z_CONST = Zone(arena, 0, 4 * 1024, "const")
    Z_BIG = Zone(arena, 4 * 1024, 64 * 1024, "big")
    Z_KV = Zone(arena, 68 * 1024, 67 * 1024, "kv")
    Z_WORK = Zone(arena, 135 * 1024, 67 * 1024, "work")

    s_idbf = Z_CONST.alloc(128, 128, BF16)
    s_idf32 = Z_CONST.alloc(128, 128, F32)
    s_mask0 = Z_CONST.alloc(128, 128, BF16)
    s_mask1 = Z_CONST.alloc(128, 128, BF16)
    s_oh32 = Z_CONST.alloc(128, 32 * 32, BF16)
    s_oh8 = Z_CONST.alloc(128, 64, BF16)
    s_eps = Z_CONST.alloc(128, 1, F32)
    s_negC = Z_CONST.alloc(128, 1, F32)

    psum = ex.enter_context(nc.psum_tensor("P", [128, 4096], F32))

    def pv(bank, banks=1, parts=128):
        return psum[0:parts, bank * 512:(bank + banks) * 512]

    def pvb(bank, parts=128):
        return psum[0:parts, bank * 512:(bank + 1) * 512].bitcast(BF16)

    r = Rec()
    _sems = {}

    def sem(name):
        if name not in _sems:
            _sems[name] = ex.enter_context(nc.semaphore(name))
        return _sems[name]


    def _finish():
        with nc.Block() as block:
            @block.sync
            def _(e):
                for f in r.ops[SY]:
                    f(e)

            @block.gpsimd
            def _(e):
                for f in r.ops[G]:
                    f(e)

            @block.scalar
            def _(e):
                for f in r.ops[SC]:
                    f(e)

            @block.vector
            def _(e):
                for f in r.ops[V]:
                    f(e)

            @block.tensor
            def _(e):
                for f in r.ops[T]:
                    f(e)
        return nc, ex

    dly = {V: Z_CONST.alloc(1, 64, F32), SC: Z_CONST.alloc(1, 64, F32)}
    r.op(V, lambda e: e.memset(dly[V], 0.0))
    r.op(SC, lambda e: e.memzero(dly[SC]))

    def delay(eng):
        buf = dly[eng]
        if eng == V:
            r.op(V, lambda e: e.tensor_copy(buf, buf))
        else:
            r.op(SC, lambda e: e.activation(buf, buf, AF.Copy))

    sC = sem("sC")
    for dst, src in ((s_idbf, d_idbf), (s_idf32, d_idf32), (s_mask0, d_mask0),
                     (s_mask1, d_mask1), (s_oh32, d_oh32), (s_oh8, d_oh8)):
        vC = r.dma(SY, dst, src[:, :], sC)
    for e in (T, V, SC, G, SY):
        r.wait(e, sC, vC)
    sEps = sem("sEps")
    r.op(V, lambda e: e.memset(s_eps, EPS))
    vEps = r.op(V, lambda e: e.memset(s_negC, -40.0), sEps)
    r.wait(SC, sEps, vEps)

    s_kfull = Z_KV.alloc(128, KVH * S, BF16)
    s_vaug = Z_KV.alloc(128, 16 * 8 * 136, BF16)
    s_big = Z_BIG.alloc(128, 32 * 1024, BF16)

    sHs = sem("sHs");   sMM = sem("sMM")
    sWs = [sem("sWs0"), sem("sWs1")]
    sEvV = sem("sEvV"); sEvS = sem("sEvS"); sRw = sem("sRw")
    sKf = [sem("sKf0"), sem("sKf1")]
    sQs = [sem("sQs0"), sem("sQs1")]
    sTp = sem("sTp");   sTb = sem("sTb")
    sUk = sem("sUk");   sUv = sem("sUv")

    tp_bank_free = {4: (None, 0), 5: (None, 0)}
    tp_rot = [0]
    tp_banks = [4, 5]

    def tp_acquire(eng):
        bk = tp_banks[tp_rot[0] % len(tp_banks)]
        tp_rot[0] += 1
        r.waitp(eng, tp_bank_free[bk])
        return bk

    def tp_release(bk, pair):
        tp_bank_free[bk] = pair

    Z_WORK.reset()
    a_ckvT = Z_WORK.alloc(128, 4 * 1024, BF16)
    a_wst = [Z_WORK.alloc(128, 4096, BF16) for _ in range(2)]
    a_cos = Z_WORK.alloc(128, 1024, F32)
    a_sin = Z_WORK.alloc(128, 1024, F32)
    a_kr = Z_WORK.alloc(128, 1024, F32)
    a_rot = Z_WORK.alloc(128, 1024, F32)
    a_t1 = Z_WORK.alloc(128, 1024, F32)
    a_t2 = Z_WORK.alloc(128, 1024, F32)
    a_kro = [Z_WORK.alloc(128, 1024, BF16) for _ in range(2)]
    a_wupk = Z_WORK.alloc(128, 4 * 512, BF16)
    a_wupv = Z_WORK.alloc(128, 4 * 1024, BF16)

    vau3 = s_vaug.rearrange("p (b x) -> p b x", x=136)
    r.op(V, lambda e: e.memset(vau3[:, :, 129:136], 0.0))
    vOne = r.op(V, lambda e: e.memset(vau3[:, :, 128:129], 1.0), sEvV)

    hsT_r = d_hsT[:, :].rearrange("(c p) t -> p c t", p=128)
    hsT3 = s_big.rearrange("p (c t) -> p c t", t=1024)

    pslot_free = [(None, 0)] * 4
    pslot_i = [0]

    def psum_acquire():
        i = pslot_i[0] % 4
        pslot_i[0] += 1
        r.waitp(T, pslot_free[i])
        return i, pv(2 * i, 2)

    wstrip_free = [(None, 0)] * 2

    def load_strip(j, dram_r, c0):
        r.waitp(G, wstrip_free[j])
        return r.dma(G, a_wst[j].rearrange("p (c r) -> p c r", r=128),
                     dram_r[:, :, c0:c0 + 128], sWs[j])

    kro_free = [(None, 0), (None, 0)]
    rot_free = (None, 0)
    tab_guard = (None, 0)

    wdk_r = d_WdkT[:, :].rearrange("(c p) r -> p c r", p=128)
    wkr_r = d_WkrT[:, :].rearrange("(c p) r -> p c r", p=128)
    wq_r = d_WqT[:, :].rearrange("(c p) r -> p c r", p=128)

    vHsTdone = {}
    for Hh in (0, 1):
        hv = 0
        for gblk in range(4):
            if Hh == 1 and gblk == 0:
                r.wait(G, sMM, vHsTdone[0])
            hv = r.dma(G, hsT3[:, 8 * gblk:8 * gblk + 8, :],
                       hsT_r[:, 8 * gblk:8 * gblk + 8, 1024 * Hh:1024 * Hh + 1024], sHs)
        r.waitp(SY, tab_guard)
        r.dma(SY, a_cos, d_cosKR[:, 1024 * Hh:1024 * Hh + 1024], sTb)
        vtab = r.dma(SY, a_sin, d_sinKR[:, 1024 * Hh:1024 * Hh + 1024], sTb)

        ckv_last = 0
        for cc in range(4):
            j = cc % 2
            vw = load_strip(j, wdk_r, 128 * cc)
            si, pp = psum_acquire()
            r.wait(T, sWs[j], vw)
            r.wait(T, sHs, hv)
            for hc in range(32):
                for nn in range(2):
                    vmm = r.op(T, lambda e, pp=pp, nn=nn, hc=hc, j=j: e.matmul(
                        pp[:, 512 * nn:512 * nn + 512],
                        a_wst[j][:, 128 * hc:128 * hc + 128],
                        s_big[:, 1024 * hc + 512 * nn:1024 * hc + 512 * nn + 512],
                        start=(hc == 0), stop=(hc == 31)),
                        sMM if (hc == 31 and nn == 1) else None)
            wstrip_free[j] = (sMM, vmm)
            r.wait(SC, sMM, vmm)
            ckv_last = r.op(SC, lambda e, pp=pp, cc=cc: e.activation(
                a_ckvT[:, 1024 * cc:1024 * cc + 1024], pp, AF.Copy), sEvS)
            pslot_free[si] = (sEvS, ckv_last)

        for cc in range(4):
            j = cc % 2
            vw = load_strip(j, wkr_r, 128 * cc)
            si, pp = psum_acquire()
            r.wait(T, sWs[j], vw)
            for hc in range(32):
                for nn in range(2):
                    vmm = r.op(T, lambda e, pp=pp, nn=nn, hc=hc, j=j: e.matmul(
                        pp[:, 512 * nn:512 * nn + 512],
                        a_wst[j][:, 128 * hc:128 * hc + 128],
                        s_big[:, 1024 * hc + 512 * nn:1024 * hc + 512 * nn + 512],
                        start=(hc == 0), stop=(hc == 31)),
                        sMM if (hc == 31 and nn == 1) else None)
            wstrip_free[j] = (sMM, vmm)
            r.wait(V, sMM, vmm)
            r.waitp(V, rot_free)
            vk = r.op(V, lambda e, pp=pp: e.tensor_copy(a_kr, pp), sEvV)
            pslot_free[si] = (sEvV, vk)
            r.wait(G, sEvV, vk)
            vr = 0
            for (d0, s0) in ((0, 32), (32, 0), (64, 96), (96, 64)):
                vr = r.dma(G, a_rot[d0:d0 + 32, :], a_kr[s0:s0 + 32, :], sRw)
            rot_free = (sRw, vr)
            if cc == 0:
                r.wait(V, sTb, vtab)
            r.wait(V, sRw, vr)
            kro = a_kro[cc % 2]
            r.waitp(V, kro_free[cc % 2])
            r.op(V, lambda e: e.tensor_mul(a_t1, a_kr, a_cos))
            r.op(V, lambda e: e.tensor_mul(a_t2, a_rot, a_sin))
            vro = r.op(V, lambda e, kro=kro: e.tensor_add(kro, a_t1, a_t2), sEvV)
            r.wait(G, sEvV, vro)
            vas = 0
            for hh in range(2):
                h = 2 * cc + hh
                base = 2048 * h + 1024 * Hh
                r.dma(G, s_kfull[0:32, base:base + 1024], kro[64 * hh:64 * hh + 32, :], sKf[cc % 2])
                vas = r.dma(G, s_kfull[64:96, base:base + 1024],
                            kro[64 * hh + 32:64 * hh + 64, :], sKf[cc % 2])
            kro_free[cc % 2] = (sKf[cc % 2], vas)
        tab_guard = (sEvV, r.cur(sEvV))

        vupk = r.dma(G, a_wupk.rearrange("p (c r) -> p c r", r=512),
                     d_WupkT[:, :].rearrange("(c p) r -> p c r", p=128), sUk)
        for kcc in range(4):
            si, pp = psum_acquire()
            if kcc == 0:
                r.wait(T, sUk, vupk)
                r.wait(T, sEvS, ckv_last)
            for cc2 in range(4):
                for nn in range(2):
                    vmm = r.op(T, lambda e, pp=pp, nn=nn, cc2=cc2, kcc=kcc: e.matmul(
                        pp[:, 512 * nn:512 * nn + 512],
                        a_wupk[:, 512 * cc2 + 128 * kcc:512 * cc2 + 128 * kcc + 128],
                        a_ckvT[:, 1024 * cc2 + 512 * nn:1024 * cc2 + 512 * nn + 512],
                        start=(cc2 == 0), stop=(cc2 == 3)),
                        sMM if (cc2 == 3 and nn == 1) else None)
            kro = a_kro[kcc % 2]
            r.wait(V, sMM, vmm)
            r.waitp(V, kro_free[kcc % 2])
            vk = r.op(V, lambda e, pp=pp, kro=kro: e.tensor_copy(kro, pp), sEvV)
            pslot_free[si] = (sEvV, vk)
            r.wait(G, sEvV, vk)
            vas = 0
            for hh in range(2):
                h = 2 * kcc + hh
                base = 2048 * h + 1024 * Hh
                r.dma(G, s_kfull[32:64, base:base + 1024], kro[64 * hh:64 * hh + 32, :], sKf[kcc % 2])
                vas = r.dma(G, s_kfull[96:128, base:base + 1024],
                            kro[64 * hh + 32:64 * hh + 64, :], sKf[kcc % 2])
            kro_free[kcc % 2] = (sKf[kcc % 2], vas)

        vupv = r.dma(G, a_wupv.rearrange("p (c r) -> p c r", r=1024),
                     d_WupvT[:, :].rearrange("(c p) r -> p c r", p=128), sUv)
        for tc in range(8):
            si, pp = psum_acquire()
            if tc == 0:
                r.wait(T, sUv, vupv)
            for cc2 in range(4):
                for nn in range(2):
                    vmm = r.op(T, lambda e, pp=pp, nn=nn, cc2=cc2, tc=tc: e.matmul(
                        pp[:, 512 * nn:512 * nn + 512],
                        a_ckvT[:, 1024 * cc2 + 128 * tc:1024 * cc2 + 128 * tc + 128],
                        a_wupv[:, 1024 * cc2 + 512 * nn:1024 * cc2 + 512 * nn + 512],
                        start=(cc2 == 0), stop=(cc2 == 3)),
                        sMM if (cc2 == 3 and nn == 1) else None)
            gtc = 8 * Hh + tc
            r.wait(SC, sMM, vmm)
            if Hh == 0 and tc == 0:
                r.wait(SC, sEvV, vOne)
            ve = r.op(SC, lambda e, pp=pp, gtc=gtc: e.copy(
                vau3[:, 8 * gtc:8 * gtc + 8, 0:128],
                pp.rearrange("p (h x) -> p h x", x=128)), sEvS)
            pslot_free[si] = (sEvS, ve)
        vHsTdone[Hh] = r.cur(sMM)

    vA1_kf = [r.cur(sKf[0]), r.cur(sKf[1])]
    vA1_evS = r.cur(sEvS)

    r.wait(G, sMM, vHsTdone[1])
    hsTq_r = d_hsTq[:, :].rearrange("(c p) t -> p c t", p=128)
    hv2 = 0
    for gblk in range(4):
        hv2 = r.dma(G, hsT3[:, 8 * gblk:8 * gblk + 8, :],
                    hsTq_r[:, 8 * gblk:8 * gblk + 8, :], sHs)
    r.waitp(SY, tab_guard)
    r.dma(SY, a_cos, d_cosQ[:, :], sTb)
    vtabq = r.dma(SY, a_sin, d_sinQ[:, :], sTb)

    for hq in range(NH):
        j = hq % 2
        vw = load_strip(j, wq_r, 128 * hq)
        si, pp = psum_acquire()
        r.wait(T, sWs[j], vw)
        if hq == 0:
            r.wait(T, sHs, hv2)
        for hc in range(32):
            for nn in range(2):
                vmm = r.op(T, lambda e, pp=pp, nn=nn, hc=hc, j=j: e.matmul(
                    pp[:, 512 * nn:512 * nn + 512],
                    a_wst[j][:, 128 * hc:128 * hc + 128],
                    s_big[:, 1024 * hc + 512 * nn:1024 * hc + 512 * nn + 512],
                    start=(hc == 0), stop=(hc == 31)),
                    sMM if (hc == 31 and nn == 1) else None)
        wstrip_free[j] = (sMM, vmm)
        r.wait(V, sMM, vmm)
        r.waitp(V, rot_free)
        vk = r.op(V, lambda e, pp=pp: e.tensor_copy(a_kr, pp), sEvV)
        pslot_free[si] = (sEvV, vk)
        r.wait(G, sEvV, vk)
        vr = 0
        for (d0, s0) in ((0, 64), (64, 0)):
            vr = r.dma(G, a_rot[d0:d0 + 64, :], a_kr[s0:s0 + 64, :], sRw)
        rot_free = (sRw, vr)
        if hq == 0:
            r.wait(V, sTb, vtabq)
        r.wait(V, sRw, vr)
        kro = a_kro[hq % 2]
        r.waitp(V, kro_free[hq % 2])
        r.op(V, lambda e: e.tensor_mul(a_t1, a_kr, a_cos))
        r.op(V, lambda e: e.tensor_mul(a_t2, a_rot, a_sin))
        vro = r.op(V, lambda e, kro=kro: e.tensor_add(kro, a_t1, a_t2), sEvV)
        r.wait(SY, sEvV, vro)
        vsp = r.dma(SY, d_qT[128 * hq:128 * hq + 128, :], kro, sQs[hq % 2])
        kro_free[hq % 2] = (sQs[hq % 2], vsp)

    if dbg:
        sdbg = sem("sdbg")
        r.wait(SY, sKf[0], vA1_kf[0])
        r.wait(SY, sKf[1], vA1_kf[1])
        r.wait(SY, sEvS, vA1_evS)
        r.dma(SY, dbg_t["kfull"][:, :], s_kfull, sdbg)
        r.dma(SY, dbg_t["vaug"][:, :], s_vaug, sdbg)

    if phases <= 1:
        for s_ in (sKf[0], sKf[1], sQs[0], sQs[1], sEvS, sEvV, sMM):
            r.wait(SY, s_, r.cur(s_))
        if dbg:
            r.wait(SY, sem("sdbg"), r.cur(sem("sdbg")))
        return _finish()

    s_ctxT = s_big
    Z_WORK.reset()
    b_qT = [Z_WORK.alloc(128, 1024, BF16) for _ in range(2)]
    b_probs = [Z_WORK.alloc(128, 512, BF16) for _ in range(2)]
    b_attn = [Z_WORK.alloc(128, 128, BF16) for _ in range(2)]
    b_rec = [Z_WORK.alloc(128, 1, F32) for _ in range(2)]
    b_sum = [Z_WORK.alloc(128, 2, F32) for _ in range(2)]
    b_dbgO = Z_WORK.alloc(128, 8 * 136, F32) if dbg else None
    b_dbgP = Z_WORK.alloc(128, 16 * 128, BF16) if dbg else None

    sQl = [sem("sQl0"), sem("sQl1")]
    sSc = sem("sSc"); sEx = sem("sEx"); sPr = sem("sPr")
    sNr = sem("sNr"); sCe = sem("sCe")

    qT_free = [(None, 0), (None, 0)]
    sum_free = [(None, 0), (None, 0)]
    probs_free = [(None, 0), (None, 0)]
    attn_free = [(None, 0), (None, 0)]
    sc_bank_free = {0: (None, 0), 1: (None, 0)}
    oc_bank_free = {2: (None, 0), 3: (None, 0)}
    sci = [0]

    r.wait(T, sKf[0], vA1_kf[0])
    r.wait(T, sKf[1], vA1_kf[1])
    r.wait(T, sEvS, vA1_evS)
    for i in range(4):
        r.waitp(T, pslot_free[i])

    r.wait(SY, sQs[0], r.cur(sQs[0]))
    r.wait(SY, sQs[1], r.cur(sQs[1]))
    pbi = [0]
    for hq in range(NH):
        kvh = hq // 4
        jq = hq % 2
        r.waitp(SY, qT_free[jq])
        vq = r.dma(SY, b_qT[jq], d_qT[128 * hq:128 * hq + 128, :], sQl[jq])
        first = True
        for lt in range(8):
            nck = 2 * lt + 2
            ob = 2 + (lt % 2)
            r.waitp(T, oc_bank_free[ob])
            for c0 in range(0, nck, 4):
                cw = min(4, nck - c0)
                sb = sci[0] % 2
                sci[0] += 1
                r.waitp(T, sc_bank_free[sb])
                if first:
                    r.wait(T, sQl[jq], vq)
                    first = False
                for jj in range(cw):
                    C = c0 + jj
                    vsc = r.op(T, lambda e, sb=sb, jj=jj, C=C, kvh=kvh, lt=lt,
                               jq=jq, cw=cw: e.matmul(
                        pv(sb)[:, 128 * jj:128 * jj + 128],
                        s_kfull[:, 2048 * kvh + 128 * C:2048 * kvh + 128 * C + 128],
                        b_qT[jq][:, 128 * lt:128 * lt + 128],
                        start=(jj == 0), stop=(jj == cw - 1)),
                        sSc if jj == cw - 1 else None)
                pb = pbi[0] % 2
                pbi[0] += 1
                r.waitp(SC, probs_free[pb])
                r.wait(SC, sSc, vsc)
                vex = r.op(SC, lambda e, pb=pb, sb=sb, cw=cw: e.activation(
                    b_probs[pb][:, 0:128 * cw], pv(sb)[:, 0:128 * cw], AF.Exp,
                    scale=SCALE, bias=s_negC), sEx)
                sc_bank_free[sb] = (sEx, vex)
                vmk = None
                for jj in range(cw):
                    C = c0 + jj
                    if C >= nck - 2:
                        mk = s_mask0 if C == nck - 2 else s_mask1
                        if vmk is None:
                            r.wait(V, sEx, vex)
                        vmk = r.op(V, lambda e, pb=pb, jj=jj, mk=mk: e.tensor_mul(
                            b_probs[pb][:, 128 * jj:128 * jj + 128],
                            b_probs[pb][:, 128 * jj:128 * jj + 128], mk), sNr)
                if dbg and hq == 27 and lt == 5:
                    sdbg2 = sem("sdbg2")
                    r.wait(V, sEx, vex)
                    if vmk is not None:
                        pass
                    r.op(V, lambda e, pb=pb, c0=c0, cw=cw: e.tensor_copy(
                        b_dbgP[:, 128 * c0:128 * (c0 + cw)],
                        b_probs[pb][:, 0:128 * cw]), sdbg2)
                r.wait(T, sEx, vex)
                if vmk is not None:
                    r.wait(T, sNr, vmk)
                for jj in range(cw):
                    C = c0 + jj
                    base = 136 * (8 * C + kvh)
                    vpr = r.op(T, lambda e, ob=ob, pb=pb, jj=jj, base=base, C=C,
                               nck=nck: e.matmul(
                        pv(ob)[:, 0:132],
                        b_probs[pb][:, 128 * jj:128 * jj + 128],
                        s_vaug[:, base:base + 132],
                        start=(C == 0), stop=(C == nck - 1)),
                        sPr if jj == cw - 1 else None)
                probs_free[pb] = (sPr, vpr)
            voc_pr = vpr
            
            ab = lt % 2
            if dbg and hq == 27:
                sdbg2 = sem("sdbg2")
                r.wait(SC, sPr, voc_pr)
                vdb = r.op(SC, lambda e, ob=ob, lt=lt: e.activation(
                    b_dbgO[:, 136 * lt:136 * lt + 132], pv(ob)[:, 0:132],
                    AF.Copy), sdbg2)
            r.wait(SC, sPr, voc_pr)
            r.waitp(SC, sum_free[ab])
            vl1 = r.op(SC, lambda e, ab=ab, ob=ob: e.activation(
                b_sum[ab][:, 0:1], pv(ob)[:, 128:129], AF.Ln), sCe)
            vrc = r.op(SC, lambda e, ab=ab: e.activation(
                b_rec[ab], b_sum[ab][:, 0:1], AF.Exp, scale=-1.0), sCe)
            r.waitp(V, attn_free[ab])
            r.wait(V, sPr, voc_pr)
            r.wait(V, sCe, vrc)
            if dbg and hq == 27:
                r.wait(V, sem("sdbg2"), r.cur(sem("sdbg2")))
            vnr = r.op(V, lambda e, ab=ab, ob=ob: e.tensor_scalar(
                b_attn[ab], pv(ob)[:, 0:128], b_rec[ab], None,
                ALU.mult), sNr)
            sum_free[ab] = (sNr, vnr)
            oc_bank_free[ob] = (sNr, vnr)
            bk = tp_acquire(T)
            r.wait(T, sNr, vnr)
            vtp = r.op(T, lambda e, bk=bk, ab=ab: e.matmul(
                pvb(bk)[:, 0:128], b_attn[ab], s_idbf,
                start=True, stop=True, is_transpose=True), sTp)
            attn_free[ab] = (sTp, vtp)
            r.wait(SC, sTp, vtp)
            vce = r.op(SC, lambda e, bk=bk, hq=hq, lt=lt: e.activation(
                s_ctxT[:, 1024 * hq + 128 * lt:1024 * hq + 128 * lt + 128],
                pvb(bk)[:, 0:128], AF.Copy), sCe)
            tp_release(bk, (sCe, vce))
        qT_free[jq] = (sPr, r.cur(sPr))
    vB_ce = r.cur(sCe)
    vB_pr = r.cur(sPr)

    if dbg:
        sdbg = sem("sdbg")
        r.wait(SY, sCe, vB_ce)
        r.dma(SY, dbg_t["ctxT"][:, :], s_ctxT, sdbg)
        r.dma(SY, dbg_t["qT"][:, :], d_qT[:, :], sdbg)
        r.wait(SY, sem("sdbg2"), r.cur(sem("sdbg2")))
        r.dma(SY, dbg_t["O"][:, :], b_dbgO, sdbg)
        r.dma(SY, dbg_t["pr"][:, :], b_dbgP, sdbg)

    if phases <= 2:
        for s_, v_ in ((sCe, r.cur(sCe)), (sNr, r.cur(sNr))):
            r.wait(SY, s_, v_)
        if dbg:
            r.wait(SY, sem("sdbg"), r.cur(sem("sdbg")))
        return _finish()

    Z_WORK.reset()
    c_wo = [Z_WORK.alloc(128, 32 * 512, BF16) for _ in range(2)]
    c_ev = [Z_WORK.alloc(128, 512, BF16) for _ in range(2)]

    sWo = [sem("sWo0"), sem("sWo1")]
    sCmm = sem("sCmm"); sCev = sem("sCev")
    sCst = [sem("sCst0"), sem("sCst1")]
    wo_r = d_WoT[:, :].rearrange("(c p) r -> p c r", p=128)
    wo_free = [(None, 0), (None, 0)]
    cev_free = [(None, 0), (None, 0)]
    cbank_free = {0: (None, 0), 1: (None, 0)}

    r.wait(T, sCe, vB_ce)
    r.wait(T, sEx, r.cur(sEx))
    r.wait(G, sCe, vB_ce)
    r.wait(G, sNr, r.cur(sNr))
    for oe in range(8):
        j = oe % 2
        r.waitp(G, wo_free[j])
        vw = r.dma(G, c_wo[j].rearrange("p (c r) -> p c r", r=512),
                   wo_r[:, :, 512 * oe:512 * oe + 512], sWo[j])
        for lt in range(8):
            cb = lt % 2
            r.waitp(T, cbank_free[cb])
            if lt == 0:
                r.wait(T, sWo[j], vw)
            for hc in range(32):
                vmm = r.op(T, lambda e, cb=cb, hc=hc, lt=lt, j=j: e.matmul(
                    pv(cb), s_ctxT[:, 1024 * hc + 128 * lt:1024 * hc + 128 * lt + 128],
                    c_wo[j][:, 512 * hc:512 * hc + 512],
                    start=(hc == 0), stop=(hc == 31)),
                    sCmm if hc == 31 else None)
            wo_free[j] = (sCmm, vmm)
            r.waitp(SC, cev_free[lt % 2])
            r.wait(SC, sCmm, vmm)
            vev = r.op(SC, lambda e, cb=cb, lt=lt: e.activation(
                c_ev[lt % 2], pv(cb), AF.Copy), sCev)
            cbank_free[cb] = (sCev, vev)
            r.wait(SY, sCev, vev)
            vst = r.dma(SY, d_attn[128 * lt:128 * lt + 128, 512 * oe:512 * oe + 512],
                        c_ev[lt % 2], sCst[lt % 2])
            cev_free[lt % 2] = (sCst[lt % 2], vst)
    vC_st = [r.cur(sCst[0]), r.cur(sCst[1])]

    s_x1T = s_big
    Z_WORK.reset()
    c2_at = Z_WORK.alloc(128, 4096, BF16)
    c2_hid = Z_WORK.alloc(128, 4096, F32)
    c2_pre = Z_WORK.alloc(128, 4096, F32)
    c2_x1 = Z_WORK.alloc(128, 4096, F32)
    c2_red = Z_WORK.alloc(128, 1, F32)
    c2_red2 = Z_WORK.alloc(128, 1, F32)
    c2_negm = Z_WORK.alloc(128, 1, F32)
    c2_sv = Z_WORK.alloc(128, 1, F32)
    c2_rstd = Z_WORK.alloc(128, 1, F32)
    Z_KV2 = Zone(arena.tensor if False else arena, 68 * 1024, 67 * 1024, "kv2")
    c2_sq = Z_KV2.alloc(128, 4096, F32)
    c2_x1b = Z_KV2.alloc(128, 4096, BF16)

    sLd = sem("sLd"); sLn = sem("sLn"); sLs = sem("sLs"); sX1 = sem("sX1")
    sXT = sem("sXT")

    def emit_ln(pre, out, sq, red, red2, negm, sv, rstd, vpre):
        r.wait(V, sLn, vpre)
        v1 = r.op(V, lambda e: e.reduce_sum(red, pre, axis=AX.X), sLn)
        r.wait(SC, sLn, v1)
        r.op(SC, lambda e: e.mul(negm, red, -1.0 / 4096.0))
        v2 = r.op(SC, lambda e: e.activation(sq, pre, AF.Square, bias=negm), sLs)
        r.wait(V, sLs, v2)
        v3 = r.op(V, lambda e: e.reduce_sum(red2, sq, axis=AX.X), sLn)
        r.wait(SC, sLn, v3)
        r.op(SC, lambda e: e.activation(sv, red2, AF.Ln, scale=1.0 / 4096.0,
                                        bias=s_eps))
        v4 = r.op(SC, lambda e: e.activation(rstd, sv, AF.Exp, scale=-0.5), sLs)
        r.wait(V, sLs, v4)
        return r.op(V, lambda e: e.tensor_scalar(out, pre, negm, rstd,
                                                 ALU.add, ALU.mult), sLn)

    x1b_guard = (None, 0)
    for lt in range(8):
        if lt == 0:
            r.wait(SY, sCst[0], vC_st[0])
            r.wait(SY, sCst[1], vC_st[1])
        va = r.dma(SY, c2_at, d_attn[128 * lt:128 * lt + 128, :], sLd)
        vh = r.dma(SY, c2_hid, d_hid[128 * lt:128 * lt + 128, :], sLd)
        r.wait(V, sLd, vh)
        r.wait(V, sLs, r.cur(sLs))
        vpre = r.op(V, lambda e: e.tensor_add(c2_pre, c2_at, c2_hid), sLn)
        v5 = emit_ln(c2_pre, c2_x1, c2_sq, c2_red, c2_red2, c2_negm, c2_sv, c2_rstd, vpre)
        r.wait(SY, sLn, v5)
        vsp = r.dma(SY, d_x1[128 * lt:128 * lt + 128, :], c2_x1, sX1)
        r.waitp(V, x1b_guard)
        vb = r.op(V, lambda e: e.tensor_copy(c2_x1b, c2_x1), sLn)
        r.wait(T, sLn, vb)
        for hc4 in range(8):
            bk = tp_acquire(T)
            for k in range(4):
                hc = 4 * hc4 + k
                vtp = r.op(T, lambda e, bk=bk, hc=hc, k=k: e.matmul(
                    pvb(bk)[:, 128 * k:128 * k + 128],
                    c2_x1b[:, 128 * hc:128 * hc + 128], s_idbf,
                    start=(k == 0), stop=(k == 3), is_transpose=True),
                    sTp if k == 3 else None)
            r.wait(SC, sTp, vtp)
            dst = s_x1T.rearrange("p (c t) -> p c t", t=1024)[
                :, 4 * hc4:4 * hc4 + 4, 128 * lt:128 * lt + 128]
            vxe = r.op(SC, lambda e, bk=bk, dst=dst: e.copy(
                dst, pvb(bk)[:, 0:512].rearrange("p (c t) -> p c t", t=128)), sXT)
            tp_release(bk, (sXT, vxe))
        x1b_guard = (sTp, r.cur(sTp))
        r.wait(V, sX1, vsp)
        r.wait(SY, sLn, v5)
    vXT_all = r.cur(sXT)

    if phases <= 3:
        r.wait(SY, sX1, r.cur(sX1))
        r.wait(SY, sXT, r.cur(sXT))
        if dbg:
            r.dma(SY, dbg_t["x1"][:, :], d_x1[:, :], sem("sdbg"))
            r.dma(SY, dbg_t["attn"][:, :], d_attn[:, :], sem("sdbg"))
            r.wait(SY, sem("sdbg"), r.cur(sem("sdbg")))
        return _finish()

    Z_WORK.reset()
    d_sst = [Z_WORK.alloc(32, 128, F32) for _ in range(2)]
    d_sTg = [Z_WORK.alloc(128, 32, F32) for _ in range(2)]
    d_sTu = [Z_WORK.alloc(128, 32, F32) for _ in range(2)]
    d_silu = Z_WORK.alloc(128, 1024, BF16)
    d_T32 = Z_WORK.alloc(32, 1024, BF16)
    d_T86 = Z_WORK.alloc(86, 1024, BF16)
    d_t8b = Z_WORK.alloc(8, 1024, BF16)
    d_zslg = Z_WORK.alloc(32, 1024, BF16)
    d_zslu = Z_WORK.alloc(32, 1024, BF16)
    Z_AL = Z_WORK.sub(49 * 1024, "alias")

    Z_AL.reset()
    d_g8b = [Z_AL.alloc(128, 4096, I8) for _ in range(2)]
    d_u8b = [Z_AL.alloc(128, 4096, I8) for _ in range(2)]
    d_dqg = Z_AL.alloc(128, 4096, BF16)
    d_dqu = Z_AL.alloc(128, 4096, BF16)
    d_lhg = Z_AL.alloc(128, 4096, BF16)
    d_lhu = Z_AL.alloc(128, 4096, BF16)
    Z_AL.reset()
    dn_d8 = [Z_AL.alloc(128, 3712, I8) for _ in range(2)]
    dn_dst = [Z_AL.alloc(86, 128, F32) for _ in range(2)]
    dn_sTd = [Z_AL.alloc(128, 86, F32) for _ in range(2)]
    dn_wdt = [Z_AL.alloc(128, 128, BF16) for _ in range(2)]
    dn_wdl = [Z_AL.alloc(128, 128, BF16) for _ in range(2)]
    dn_ydump = [Z_AL.alloc(128, 1024, F32) for _ in range(2)]
    dn_yin = Z_AL.alloc(128, 1024, F32)
    dn_ytile = Z_AL.alloc(128, 1024, F32)
    Z_AL.reset()
    zp_a = Z_AL.alloc(128, 1376, F32)
    zp_b = Z_AL.alloc(128, 1376, F32)
    zp_c = Z_AL.alloc(128, 1376, BF16)

    Z_KV2.reset()
    d_hslab = Z_KV2.alloc(128, 29 * 1024, BF16)
    d_zsd = Z_KV2.alloc(86, 4096, BF16)

    sG8 = [sem("sG8a"), sem("sG8b")]
    sU8 = [sem("sU8a"), sem("sU8b")]
    sSst = [sem("sSsta"), sem("sSstb")]
    sD8 = [sem("sD8a"), sem("sD8b")]
    sDst = [sem("sDsta"), sem("sDstb")]
    sDqV = sem("sDqV"); sDqS = sem("sDqS"); sDt = sem("sDt")
    sDe = sem("sDe"); sDm = sem("sDm"); sDh = sem("sDh"); sDz = sem("sDz")
    sDsc = sem("sDsc")
    sT8 = sem("sT8"); sT8d = sem("sT8d"); sDy = sem("sDy")
    sYp = [sem("sYpa"), sem("sYpb")]
    sYi = sem("sYi"); sYn = sem("sYn"); sZp = sem("sZp")

    for (dz_, dsc_, dst_, nstr) in ((d_gz, d_gsc, d_zsg, 8), (d_uz, d_usc, d_zsu, 8)):
        for st_ in range(nstr):
            sl = slice(1376 * st_, 1376 * st_ + 1376)
            r.dma(SY, zp_a[0:32, :], dz_[:, sl], sZp)
            v2 = r.dma(SY, zp_b[0:32, :], dsc_[:, sl], sZp)
            r.wait(V, sZp, v2)
            vz = r.op(V, lambda e: e.tensor_mul(zp_c[0:32, :], zp_a[0:32, :],
                                                zp_b[0:32, :]), sLn)
            r.wait(SY, sLn, vz)
            vzs = r.dma(SY, dst_[:, sl], zp_c[0:32, :], sZp)
            r.wait(SY, sZp, vzs)
    for st_ in range(4):
        sl = slice(1024 * st_, 1024 * st_ + 1024)
        r.dma(SY, zp_a[0:86, 0:1024], d_dz[:, sl], sZp)
        v2 = r.dma(SY, zp_b[0:86, 0:1024], d_dsc[:, sl], sZp)
        r.wait(V, sZp, v2)
        vzsd = r.op(V, lambda e, sl=sl: e.tensor_mul(d_zsd[:, sl], zp_a[0:86, 0:1024],
                                                     zp_b[0:86, 0:1024]), sLn)
        if st_ < 3:
            r.wait(SY, sLn, vzsd)

    r.wait(T, sXT, vXT_all)
    r.wait(T, sCev, r.cur(sCev))
    for g in range(32):
        for nn in range(2):
            vt32 = r.op(T, lambda e, g=g, nn=nn: e.matmul(
                pv(6 + nn, parts=32),
                s_oh32[:, 32 * g:32 * g + 32],
                s_x1T[:, 1024 * g + 512 * nn:1024 * g + 512 * nn + 512],
                start=(g == 0), stop=(g == 31)),
                sDm if (g == 31 and nn == 1) else None)
    r.wait(V, sDm, vt32)
    vT32 = r.op(V, lambda e: e.tensor_scalar(d_T32, pv(6, 2, parts=32), -1.0, None,
                                             ALU.mult), sLn)

    g8_free = [(None, 0), (None, 0)]
    u8_free = [(None, 0), (None, 0)]
    sst_free = [(None, 0), (None, 0)]
    sTg_free = [(None, 0), (None, 0)]
    sTu_free = [(None, 0), (None, 0)]
    silu_free = (None, 0)
    dq_free = {"g": [(None, 0), (None, 0)], "u": [(None, 0), (None, 0)]}
    lh_free = {"g": (None, 0), "u": (None, 0)}
    zsl_free = (None, 0)
    vDh_last = 0
    down_done = []
    h_guard = (None, 0)
    vZfin = r.cur(sZp)

    for si_, (s0, sn) in enumerate(SLABS):
        for ci in range(sn):
            t = s0 + ci
            jw = t % 2
            if t % 8 == 0:
                r.waitp(SY, zsl_free)
                n8 = min(8, 86 - t)
                r.dma(SY, d_zslg[0:32, 0:128 * n8], d_zsg[:, 128 * t:128 * (t + n8)], sDz)
                if t == 0:
                    r.wait(SY, sZp, vZfin)
                vzl = r.dma(SY, d_zslu[0:32, 0:128 * n8], d_zsu[:, 128 * t:128 * (t + n8)], sDz)
            r.waitp(SY, g8_free[jw])
            r.waitp(SY, u8_free[jw])
            if ci < 2:
                for pr in down_done:
                    r.waitp(SY, pr)
            vg8 = r.dma(SY, d_g8b[jw], d_G8[128 * t:128 * t + 128, :], sG8[jw])
            vu8 = r.dma(SY, d_u8b[jw], d_U8[128 * t:128 * t + 128, :], sU8[jw])
            r.waitp(SY, sst_free[jw])
            vs1 = r.dma(SY, d_sst[jw][0:32, :], d_gsc[:, 128 * t:128 * t + 128], sSst[jw])
            bk = tp_acquire(T)
            r.wait(T, sSst[jw], vs1)
            r.waitp(T, sTg_free[jw])
            vtg = r.op(T, lambda e, bk=bk, jw=jw: e.matmul(
                pv(bk)[:, 0:32], d_sst[jw][0:32, :], s_idf32[0:32, 0:32],
                start=True, stop=True, is_transpose=True), sDt)
            r.wait(SC, sDt, vtg)
            delay(SC)
            veg = r.op(SC, lambda e, bk=bk, jw=jw: e.activation(
                d_sTg[jw], pv(bk)[:, 0:32], AF.Copy), sDsc)
            tp_release(bk, (sDsc, veg))
            r.wait(SY, sDt, vtg)
            vs2 = r.dma(SY, d_sst[jw][0:32, :], d_usc[:, 128 * t:128 * t + 128], sSst[jw])
            bk2 = tp_acquire(T)
            r.wait(T, sSst[jw], vs2)
            r.waitp(T, sTu_free[jw])
            vtu = r.op(T, lambda e, bk2=bk2, jw=jw: e.matmul(
                pv(bk2)[:, 0:32], d_sst[jw][0:32, :], s_idf32[0:32, 0:32],
                start=True, stop=True, is_transpose=True), sDt)
            sst_free[jw] = (sDt, vtu)
            r.wait(SC, sDt, vtu)
            delay(SC)
            veu = r.op(SC, lambda e, bk2=bk2, jw=jw: e.activation(
                d_sTu[jw], pv(bk2)[:, 0:32], AF.Copy), sDsc)
            tp_release(bk2, (sDsc, veu))

            r.wait(V, sG8[jw], vg8)
            r.wait(V, sDsc, veg)
            r.wait(SC, sU8[jw], vu8)
            r.wait(SC, sDsc, veu)
            vdqg = {}
            vdqu = {}
            for half in range(2):
                for gg in range(16):
                    g = 16 * half + gg
                    last = (gg == 15)
                    vg_ = r.op(V, lambda e, jw=jw, g=g: e.tensor_scalar(
                        d_dqg[:, 128 * g:128 * g + 128],
                        d_g8b[jw][:, 128 * g:128 * g + 128],
                        d_sTg[jw][:, g:g + 1], None, ALU.mult),
                        sDqV if last else None)
                    vu_ = r.op(SC, lambda e, jw=jw, g=g: e.mul(
                        d_dqu[:, 128 * g:128 * g + 128],
                        d_u8b[jw][:, 128 * g:128 * g + 128],
                        d_sTu[jw][:, g:g + 1]),
                        sDqS if last else None)
                vdqg[half] = vg_
                vdqu[half] = vu_
            g8_free[jw] = (sDqV, vdqg[1])
            u8_free[jw] = (sDqS, vdqu[1])
            sTg_free[jw] = (sDqV, vdqg[1])
            sTu_free[jw] = (sDqS, vdqu[1])

            ev_last = {}
            for (mat, dq, lh, ev_eng, vdq_) in (("g", d_dqg, d_lhg, V, vdqg),
                                               ("u", d_dqu, d_lhu, SC, vdqu)):
                sdq_ = sDqV if mat == "g" else sDqS
                r.waitp(ev_eng, lh_free[mat])
                for b4 in range(8):
                    bk = tp_acquire(T)
                    if b4 == 0:
                        r.wait(T, sdq_, vdq_[0])
                    elif b4 == 4:
                        r.wait(T, sdq_, vdq_[1])
                    for k in range(4):
                        g = 4 * b4 + k
                        vtp = r.op(T, lambda e, bk=bk, k=k, g=g, dq=dq: e.matmul(
                            pvb(bk)[:, 128 * k:128 * k + 128],
                            dq[:, 128 * g:128 * g + 128], s_idbf,
                            start=(k == 0), stop=(k == 3), is_transpose=True),
                            sDt if k == 3 else None)
                    r.wait(ev_eng, sDt, vtp)
                    vde = r.op(ev_eng, lambda e, bk=bk, b4=b4, lh=lh: (
                        e.tensor_copy if hasattr(e, "tensor_copy") else e.copy)(
                        lh[:, 512 * b4:512 * b4 + 512], pvb(bk)[:, 0:512]), sDe)
                    tp_release(bk, (sDe, vde))
                ev_last[mat] = vde

            r.wait(T, sDe, max(ev_last["g"], ev_last["u"]))
            if si_ == 0 and ci == 0:
                r.wait(T, sLn, vT32)
            if t % 8 == 0:
                r.wait(T, sDz, vzl)
            for (mat, lh, pb, zsl) in (("g", d_lhg, 0, d_zslg), ("u", d_lhu, 2, d_zslu)):
                for g in range(32):
                    for nn in range(2):
                        r.op(T, lambda e, lh=lh, pb=pb, g=g, nn=nn: e.matmul(
                            pv(pb + nn), lh[:, 128 * g:128 * g + 128],
                            s_x1T[:, 1024 * g + 512 * nn:1024 * g + 512 * nn + 512],
                            start=(g == 0), stop=False))
                for nn in range(2):
                    vgm = r.op(T, lambda e, pb=pb, nn=nn, zsl=zsl, t=t: e.matmul(
                        pv(pb + nn), zsl[0:32, 128 * (t % 8):128 * (t % 8) + 128],
                        d_T32[:, 512 * nn:512 * nn + 512],
                        start=False, stop=True),
                        sDm if (pb == 2 and nn == 1) else None)
            zsl_free = (sDm, vgm)
            lh_free["g"] = (sDm, vgm)
            lh_free["u"] = (sDm, vgm)

            r.waitp(SC, silu_free)
            r.wait(SC, sDm, vgm)
            vsl = r.op(SC, lambda e: e.activation(d_silu, pv(0, 2), AF.Silu), sDsc)
            r.wait(V, sDsc, vsl)
            if ci == 0:
                r.waitp(V, h_guard)
            vh = r.op(V, lambda e, ci=ci: e.tensor_mul(
                d_hslab[:, 1024 * ci:1024 * ci + 1024], d_silu, pv(2, 2)), sDh)
            silu_free = (sDh, vh)
            vDh_last = vh
            r.wait(T, sDh, vh)
            r.op(T, lambda e, t=t, ci=ci: e.matmul(
                pv(6, parts=8), s_oh8[:, 8 * (t % 8):8 * (t % 8) + 8],
                d_hslab[:, 1024 * ci:1024 * ci + 512],
                start=(t % 8 == 0), stop=(t % 8 == 7 or t == 85)))
            vt8m = r.op(T, lambda e, t=t, ci=ci: e.matmul(
                pv(7, parts=8), s_oh8[:, 8 * (t % 8):8 * (t % 8) + 8],
                d_hslab[:, 1024 * ci + 512:1024 * ci + 1024],
                start=(t % 8 == 0), stop=(t % 8 == 7 or t == 85)), sT8)
            if t % 8 == 7 or t == 85:
                nrow = 8 if t % 8 == 7 else 6
                r.wait(V, sT8, vt8m)
                vt8e = r.op(V, lambda e, nrow=nrow: e.tensor_scalar(
                    d_t8b[0:nrow, :], pv(6, 2, parts=8)[0:nrow, :], -1.0, None,
                    ALU.mult), sT8)
                r.wait(G, sT8, vt8e)
                r0 = (t // 8) * 8
                r.dma(G, d_T86[r0:r0 + nrow, :], d_t8b[0:nrow, :], sT8d)

        if dbg and si_ == 0:
            sdbg = sem("sdbg")
            r.wait(SY, sDh, vDh_last)
            r.dma(SY, dbg_t["h0"][:, :], d_hslab[:, 0:2048], sdbg)

        last_slab = (si_ == len(SLABS) - 1)
        dn_free = {0: (None, 0), 2: (None, 0)}
        ypart_done = getattr(build, "_ypd", None) or [None] * 32
        build._ypd = ypart_done
        d8_free = [(None, 0), (None, 0)]
        dst_free = [(None, 0), (None, 0)]
        sTd_free = [(None, 0), (None, 0)]
        wdt_free = [(None, 0), (None, 0)]
        wdl_free = [(None, 0), (None, 0)]
        ydump_free = [(None, 0), (None, 0)]
        yin_free = (None, 0)
        ytile_free = (None, 0)
        r.wait(SY, sDqV, r.cur(sDqV))
        r.wait(SY, sDqS, r.cur(sDqS))
        r.wait(SY, sDe, r.cur(sDe))
        r.wait(T, sDh, vDh_last)

        for oc in range(32):
            jo = oc % 2
            pb = 2 * (oc % 2)
            r.waitp(SY, d8_free[jo])
            vd8 = r.dma(SY, dn_d8[jo][:, 0:128 * sn],
                        d_D8[128 * oc:128 * oc + 128, 128 * s0:128 * (s0 + sn)], sD8[jo])
            r.waitp(SY, dst_free[jo])
            vds = r.dma(SY, dn_dst[jo], d_dsc[:, 128 * oc:128 * oc + 128], sDst[jo])
            bk = tp_acquire(T)
            r.wait(T, sDst[jo], vds)
            vtd = r.op(T, lambda e, bk=bk, jo=jo: e.matmul(
                pv(bk)[:, 0:86], dn_dst[jo], s_idf32[0:86, 0:86],
                start=True, stop=True, is_transpose=True), sDt)
            dst_free[jo] = (sDt, vtd)
            r.wait(SC, sDt, vtd)
            r.waitp(SC, sTd_free[jo])
            delay(SC)
            vsd = r.op(SC, lambda e, bk=bk, jo=jo: e.activation(
                dn_sTd[jo], pv(bk)[:, 0:86], AF.Copy), sDsc)
            tp_release(bk, (sDsc, vsd))
            r.waitp(T, dn_free[pb])
            r.wait(V, sD8[jo], vd8)
            r.wait(V, sDsc, vsd)
            prev_vtw = [None, None]
            for ci in range(sn):
                t = s0 + ci
                jt = ci % 2
                r.waitp(V, wdt_free[jt])
                r.waitp(V, wdl_free[jt])
                vdq = r.op(V, lambda e, jo=jo, jt=jt, ci=ci, t=t: e.tensor_scalar(
                    dn_wdt[jt], dn_d8[jo][:, 128 * ci:128 * ci + 128],
                    dn_sTd[jo][:, t:t + 1], None, ALU.mult), sDqV)
                bk = tp_acquire(T)
                r.wait(T, sDqV, vdq)
                vtw = r.op(T, lambda e, bk=bk, jt=jt: e.matmul(
                    pvb(bk)[:, 0:128], dn_wdt[jt], s_idbf,
                    start=True, stop=True, is_transpose=True), sDt)
                wdt_free[jt] = (sDt, vtw)
                r.wait(V, sDt, vtw)
                vle = r.op(V, lambda e, jt=jt, bk=bk: e.tensor_copy(
                    dn_wdl[jt], pvb(bk)[:, 0:128]), sDe)
                tp_release(bk, (sDe, vle))
                if prev_vtw[1 - jt] is not None:
                    wdl_free[jt] = (sDt, prev_vtw[1 - jt])
                prev_vtw[jt] = vtw
                r.wait(T, sDe, vle)
                for nn in range(2):
                    vdn = r.op(T, lambda e, pb=pb, jt=jt, ci=ci, nn=nn, sn=sn,
                               last_slab=last_slab: e.matmul(
                        pv(pb + nn), dn_wdl[jt],
                        d_hslab[:, 1024 * ci + 512 * nn:1024 * ci + 512 * nn + 512],
                        start=(ci == 0), stop=(ci == sn - 1
                                               and not last_slab)),
                        sDm if (ci == sn - 1 and nn == 1 and not last_slab) else None)
            d8_free[jo] = (sDqV, r.cur(sDqV))
            sTd_free[jo] = (sDqV, r.cur(sDqV))
            if last_slab:
                r.wait(T, sT8d, r.cur(sT8d))
                for nn in range(2):
                    vdn = r.op(T, lambda e, pb=pb, oc=oc, nn=nn: e.matmul(
                        pv(pb + nn), d_zsd[:, 128 * oc:128 * oc + 128],
                        d_T86[:, 512 * nn:512 * nn + 512],
                        start=False, stop=True),
                        sDm if nn == 1 else None)
            wdl_free[0] = (sDm, vdn)
            wdl_free[1] = (sDm, vdn)
            jy = oc % 2
            r.waitp(V, ydump_free[jy])
            r.wait(V, sDm, vdn)
            if si_ == 0:
                vev = r.op(V, lambda e, pb=pb, jy=jy: e.tensor_copy(
                    dn_ydump[jy], pv(pb, 2)), sDy)
                dn_free[pb] = (sDy, vev)
                r.wait(SY, sDy, vev)
                vyd = r.dma(SY, d_ypart[128 * oc:128 * oc + 128, :], dn_ydump[jy], sYp[jy])
                ydump_free[jy] = (sYp[jy], vyd)
                ypart_done[oc] = (sYp[jy], vyd)
            else:
                r.waitp(SY, yin_free)
                r.waitp(SY, ypart_done[oc])
                vyi = r.dma(SY, dn_yin, d_ypart[128 * oc:128 * oc + 128, :], sYi)
                r.wait(V, sYi, vyi)
                vev = r.op(V, lambda e, pb=pb, jy=jy: e.scalar_tensor_tensor(
                    dn_ydump[jy], pv(pb, 2), 1.0, dn_yin, ALU.mult, ALU.add), sDy)
                dn_free[pb] = (sDy, vev)
                yin_free = (sDy, vev)
                if not last_slab:
                    r.wait(SY, sDy, vev)
                    vyd = r.dma(SY, d_ypart[128 * oc:128 * oc + 128, :], dn_ydump[jy], sYp[jy])
                    ydump_free[jy] = (sYp[jy], vyd)
                    ypart_done[oc] = (sYp[jy], vyd)
                else:
                    r.waitp(V, ytile_free)
                    vye = 0
                    for lt4 in range(2):
                        bk = tp_acquire(T)
                        r.wait(T, sDy, vev)
                        for k in range(4):
                            lt = 4 * lt4 + k
                            vty = r.op(T, lambda e, bk=bk, k=k, lt=lt, jy=jy: e.matmul(
                                pv(bk)[:, 128 * k:128 * k + 128],
                                dn_ydump[jy][:, 128 * lt:128 * lt + 128],
                                s_idf32, start=(k == 0), stop=(k == 3),
                                is_transpose=True),
                                sDt if k == 3 else None)
                        r.wait(V, sDt, vty)
                        vye = r.op(V, lambda e, bk=bk, lt4=lt4: e.tensor_copy(
                            dn_ytile[:, 512 * lt4:512 * lt4 + 512], pv(bk)), sDy)
                        tp_release(bk, (sDy, vye))
                    ydump_free[jy] = (sDy, vye)
                    r.wait(SY, sDy, vye)
                    vyn = 0
                    for lt in range(8):
                        vyn = r.dma(SY, d_ynat[128 * lt:128 * lt + 128,
                                               128 * oc:128 * oc + 128],
                                    dn_ytile[:, 128 * lt:128 * lt + 128], sYn)
                    ytile_free = (sYn, vyn)
        down_done = [(sDy, r.cur(sDy)), (sYp[0], r.cur(sYp[0])),
                     (sYp[1], r.cur(sYp[1])), (sYi, r.cur(sYi)),
                     (sYn, r.cur(sYn)), (sDqV, r.cur(sDqV)),
                     (sDt, r.cur(sDt)), (sDm, r.cur(sDm))]
        h_guard = (sDm, r.cur(sDm))
    vYn = r.cur(sYn)

    if phases <= 4:
        for s_ in (sYn, sYp[0], sYp[1], sT8d):
            r.wait(SY, s_, r.cur(s_))
        if dbg:
            r.wait(SY, sem("sdbg"), r.cur(sem("sdbg")))
        return _finish()

    Z_WORK.reset()
    e_y = Z_WORK.alloc(128, 4096, F32)
    e_x1 = Z_WORK.alloc(128, 4096, F32)
    e_pre = Z_WORK.alloc(128, 4096, F32)
    Z_KV2.reset()
    e_sq = Z_KV2.alloc(128, 4096, F32)
    e_out = Z_KV2.alloc(128, 4096, F32)
    e_red = Z_WORK.alloc(128, 1, F32)
    e_red2 = Z_WORK.alloc(128, 1, F32)
    e_negm = Z_WORK.alloc(128, 1, F32)
    e_sv = Z_WORK.alloc(128, 1, F32)
    e_rstd = Z_WORK.alloc(128, 1, F32)
    sOut = sem("sOut")
    r.wait(SY, sYn, vYn)
    r.wait(SY, sX1, r.cur(sX1))
    for pr in down_done:
        r.waitp(SY, pr)
    eout_guard = (None, 0)
    for lt in range(8):
        vy = r.dma(SY, e_y, d_ynat[128 * lt:128 * lt + 128, :], sLd)
        vx = r.dma(SY, e_x1, d_x1[128 * lt:128 * lt + 128, :], sLd)
        r.wait(V, sLd, vx)
        r.wait(V, sLs, r.cur(sLs))
        vpre = r.op(V, lambda e: e.tensor_add(e_pre, e_y, e_x1), sLn)
        r.waitp(V, eout_guard)
        v5 = emit_ln(e_pre, e_out, e_sq, e_red, e_red2, e_negm, e_sv, e_rstd, vpre)
        r.wait(SY, sLn, v5)
        vo = r.dma(SY, d_out[128 * lt:128 * lt + 128, :], e_out, sOut)
        eout_guard = (sOut, vo)
        r.wait(SY, sLn, v5)
    r.wait(SY, sOut, r.cur(sOut))

    return _finish()


def _unpack4(q):
    out = np.empty((q.shape[0], q.shape[1] * 8), dtype=np.uint8)
    for i in range(8):
        out[:, i::8] = ((q >> (4 * i)) & 15).astype(np.uint8)
    return out


def _host_prep(inputs):
    hs = np.asarray(inputs["L0_attn_hidden_states"], dtype=np.float32)

    def tr(x):
        return np.ascontiguousarray(np.asarray(x, np.float32).T)

    WqT, WkrT, WdkT = tr(inputs["L0_attn_Wq"]), tr(inputs["L0_attn_Wkr"]), tr(inputs["L0_attn_Wdk"])
    WupkT, WupvT, WoT = tr(inputs["L0_attn_Wupk"]), tr(inputs["L0_attn_Wupv"]), tr(inputs["L0_attn_Wo"])

    G8 = np.ascontiguousarray(_unpack4(np.asarray(inputs["L0_ffn_gate_qweight"])).T).view(np.int8)
    U8 = np.ascontiguousarray(_unpack4(np.asarray(inputs["L0_ffn_up_qweight"])).T).view(np.int8)
    D8 = np.ascontiguousarray(_unpack4(np.asarray(inputs["L0_ffn_down_qweight"])).T).view(np.int8)
    gz = _unpack4(np.asarray(inputs["L0_ffn_gate_qzeros"])).astype(np.float32)
    uz = _unpack4(np.asarray(inputs["L0_ffn_up_qzeros"])).astype(np.float32)
    dz = _unpack4(np.asarray(inputs["L0_ffn_down_qzeros"])).astype(np.float32)
    gsc = np.asarray(inputs["L0_ffn_gate_scales"], np.float32)
    usc = np.asarray(inputs["L0_ffn_up_scales"], np.float32)
    dsc = np.asarray(inputs["L0_ffn_down_scales"], np.float32)

    inv_freq = (1.0 / (10000.0 ** (np.arange(0, HD, 2, dtype=np.float32) / HD)))
    pos = np.arange(S, dtype=np.float32)
    freqs = np.outer(pos, inv_freq)
    emb = np.concatenate([freqs, freqs], axis=-1)
    cosT = np.ascontiguousarray(np.cos(emb).astype(np.float32).T)
    sinT = np.ascontiguousarray(np.sin(emb).astype(np.float32).T)
    sgnQ = np.where(np.arange(HD) < 64, -1.0, 1.0).astype(np.float32)[:, None]
    sinQS = sinT * sgnQ
    dd = np.arange(128) % 64
    dd = np.where(dd < 32, dd, dd + 32)
    cosKR = np.ascontiguousarray(cosT[dd])
    sgnK = np.where(np.arange(128) % 64 < 32, -1.0, 1.0).astype(np.float32)[:, None]
    sinKR = np.ascontiguousarray(sinT[dd] * sgnK)

    idbf = np.eye(128, dtype=BF)
    idf32 = np.eye(128, dtype=np.float32)
    oh32 = np.zeros((128, 32 * 32), dtype=BF)
    for g in range(32):
        oh32[:, 32 * g + g] = 1.0
    oh8 = np.zeros((128, 64), dtype=BF)
    for g in range(8):
        oh8[:, 8 * g + g] = 1.0

    shared = dict(WqT=WqT, WkrT=WkrT, WdkT=WdkT, WupkT=WupkT, WupvT=WupvT, WoT=WoT,
                  G8=G8, U8=U8, D8=D8, gsc=gsc, usc=usc, dsc=dsc,
                  gz=gz, uz=uz, dz=dz, cosKR=cosKR, sinKR=sinKR,
                  idbf=idbf.view(np.uint16), idf32=idf32,
                  oh32=oh32.view(np.uint16), oh8=oh8.view(np.uint16))

    in_maps = []
    for c in range(8):
        b, p = c // 2, c % 2
        qi = np.arange(128)[None, :]
        kc = np.arange(128)[:, None]
        m0 = (kc <= 2 * qi + p).astype(BF)
        m1 = (kc <= 2 * qi + p - 128).astype(BF)
        own = np.arange(p, S, 2)
        m = dict(shared)
        m.update(hsT=np.ascontiguousarray(hs[b].T),
                 hsTq=np.ascontiguousarray(hs[b, p::2].T),
                 hid_own=np.ascontiguousarray(hs[b, p::2]),
                 cosQ=np.ascontiguousarray(cosT[:, own]),
                 sinQ=np.ascontiguousarray(sinQS[:, own]),
                 mask0=m0.view(np.uint16), mask1=m1.view(np.uint16))
        in_maps.append(m)
    return in_maps


def kernel(**inputs):
    in_maps = _host_prep(inputs)
    nc, ex = build(dbg=False)
    try:
        res = run_bass_kernel_spmd(nc, in_maps, core_ids=list(range(8)))
    finally:
        ex.close()
    out = np.empty((B, S, H), dtype=np.float32)
    for c in range(8):
        b, p = c // 2, c % 2
        out[b, p::2, :] = res.results[c]["out"]
    return out
